# version 34 (final)
# baseline (speedup 1.0000x reference)
import numpy as np
import ml_dtypes
import concourse.bass as bass
import concourse.mybir as mybir
from concourse.alu_op_type import AluOpType
from concourse.bass_utils import run_bass_kernel_spmd
from contextlib import ExitStack

F32 = mybir.dt.float32
BF16 = mybir.dt.bfloat16
AF = mybir.ActivationFunctionType
AX = mybir.AxisListType
OP = AluOpType

NCORE = 8
T = 2048
D = 1024
NT = 16
NC = 8
TG = 512
EPS = 1e-5


def build_nc(dbg=()):
    nc = bass.Bass("TRN2", target_bir_lowering=False, debug=False)

    x_d = nc.dram_tensor("x", [T, D], F32, kind="ExternalInput").ap()
    wk_d = nc.dram_tensor("wk", [128, 8192], F32, kind="ExternalInput").ap()
    wka_d = nc.dram_tensor("wka", [128, 8192], F32, kind="ExternalInput").ap()
    wv_d = nc.dram_tensor("wv", [128, 8192], F32, kind="ExternalInput").ap()
    smalls_d = nc.dram_tensor("smalls", [128, 112], F32, kind="ExternalInput").ap()
    eye32_d = nc.dram_tensor("eye32", [128, 128], F32, kind="ExternalInput").ap()
    eyebf_d = nc.dram_tensor("eyebf", [128, 128], BF16, kind="ExternalInput").ap()
    onesbf_d = nc.dram_tensor("onesbf", [1, 128], BF16, kind="ExternalInput").ap()
    g1r_d = nc.dram_tensor("g1r", [4, 1024], F32, kind="ExternalInput").ap()
    g2r_d = nc.dram_tensor("g2r", [4, 256], F32, kind="ExternalInput").ap()
    g1i_d = nc.dram_tensor("g1i", [4, 1024], F32, kind="ExternalInput").ap()
    g2i_d = nc.dram_tensor("g2i", [4, 256], F32, kind="ExternalInput").ap()
    bcat_d = nc.dram_tensor("bcat", [1, 1024], F32, kind="ExternalInput").ap()
    out_d = nc.dram_tensor("out", [T, D], F32, kind="ExternalOutput").ap()

    wrt_d = nc.dram_tensor("wrt_s", [512, 512], BF16)
    wit_d = nc.dram_tensor("wit_s", [512, 512], BF16)
    witn_d = nc.dram_tensor("witn_s", [512, 512], BF16)
    cca_in = nc.dram_tensor("cca_in", [128, 8], F32)
    cca_out = nc.dram_tensor("cca_out", [1024, 8], F32)
    ccp_in = nc.dram_tensor("ccp_in", [128, 8], F32)
    ccp_out = nc.dram_tensor("ccp_out", [1024, 8], F32)

    dbg_out = {}
    for name, shape in [
        ("dbg_a", [128, NC * T]), ("dbg_k", [128, NC * T]), ("dbg_v", [128, NC * T]),
        ("dbg_ta", [128, 8]), ("dbg_pa", [128, 8]), ("dbg_pk", [128, NC * T]),
        ("dbg_o", [128, NC * T]), ("dbg_n", [128, NC * T]), ("dbg_m", [128, 8192]),
        ("dbg_tp", [128, 8]), ("dbg_pp", [128, 8]),
    ]:
        if name in dbg:
            dt = BF16 if name in ("dbg_k", "dbg_v", "dbg_n", "dbg_m") else F32
            dbg_out[name] = nc.dram_tensor(name, shape, dt, kind="ExternalOutput").ap()

    with ExitStack() as ctx:
        E = ctx.enter_context
        U1 = E(nc.sbuf_tensor("U1", [128, 32768], BF16))
        abuf = E(nc.sbuf_tensor("abuf", [128, NC * T], F32))
        nfm = E(nc.sbuf_tensor("nfm", [128, NC * T], BF16))
        wkT = E(nc.sbuf_tensor("wkT", [128, 8192], BF16))
        wkaT = E(nc.sbuf_tensor("wkaT", [128, 8192], BF16))
        smalls = E(nc.sbuf_tensor("smalls_s", [128, 112], F32))
        eye32 = E(nc.sbuf_tensor("eye32_s", [128, 128], F32))
        eyebf = E(nc.sbuf_tensor("eyebf_s", [128, 128], BF16))
        onesbf = E(nc.sbuf_tensor("onesbf_s", [1, 128], BF16))
        g1r = E(nc.sbuf_tensor("g1r_s", [4, 1024], BF16))
        g2r = E(nc.sbuf_tensor("g2r_s", [4, 256], BF16))
        g1i = E(nc.sbuf_tensor("g1i_s", [4, 1024], BF16))
        g2i = E(nc.sbuf_tensor("g2i_s", [4, 256], BF16))
        bcat = E(nc.sbuf_tensor("bcat_s", [1, 1024], BF16))
        t4a = E(nc.sbuf_tensor("t4a", [128, 512], BF16))
        t4b = E(nc.sbuf_tensor("t4b", [128, 512], BF16))
        t4n = E(nc.sbuf_tensor("t4n", [128, 512], BF16))
        bka32 = E(nc.sbuf_tensor("bka32", [128, 8], F32))
        bnst = E(nc.sbuf_tensor("bnst", [128, 12], F32))
        smv = E(nc.sbuf_tensor("smv", [128, 32], F32))
        srs1 = E(nc.sbuf_tensor("srs1", [128, 16], F32))
        slntmp = E(nc.sbuf_tensor("slntmp", [128, 16], F32))
        expacc = E(nc.sbuf_tensor("expacc", [128, 32], F32))
        sTA = E(nc.sbuf_tensor("sTA", [128, 8], F32))
        sTP = E(nc.sbuf_tensor("sTP", [128, 8], F32))
        sPa = E(nc.sbuf_tensor("sPa", [128, 8], F32))
        sPp = E(nc.sbuf_tensor("sPp", [128, 8], F32))
        g_a = E(nc.sbuf_tensor("g_a", [128, 64], F32))
        g_p = E(nc.sbuf_tensor("g_p", [128, 64], F32))
        g_m = E(nc.sbuf_tensor("g_m", [128, 64], F32))
        smv2 = E(nc.sbuf_tensor("smv2", [128, 32], F32))
        srs2 = E(nc.sbuf_tensor("srs2", [128, 16], F32))
        smv3 = E(nc.sbuf_tensor("smv3", [128, 32], F32))
        srs3 = E(nc.sbuf_tensor("srs3", [128, 16], F32))
        bnst2 = E(nc.sbuf_tensor("bnst2", [128, 12], F32))
        bnst3 = E(nc.sbuf_tensor("bnst3", [128, 12], F32))

        pm = E(nc.psum_tensor("pm", [128, 512], F32))
        pt0 = E(nc.psum_tensor("pt0", [128, 1024], F32))
        pt1 = E(nc.psum_tensor("pt1", [128, 1024], F32))
        pf = E(nc.psum_tensor("pf", [128, 1024], F32))
        py1 = E(nc.psum_tensor("py1", [128, 512], F32))

        sem_names = [
            "qx0", "qx1", "qx2s", "qx3", "qx4", "qx5",
            "qw", "qs", "qm1", "qm1done", "qm2",
            "qxa", "qxb", "qxc", "qxd", "qsta", "qstb", "qg",
            "sM1", "sMev", "sTx", "sWev", "sNT", "sNev",
            "sKmm", "sKev", "sVmm", "sVev", "sKAmm", "sAev",
            "vBn", "vMV", "sRs1", "vNorm", "vInit",
            "vTa", "vPa", "vScanA", "sRA", "vPk", "vTp", "vPp", "vO",
            "sOT", "vR1", "sSq2", "vVar2", "sRs2", "vY1", "sY1bf",
            "sY1T", "sY1ev", "sFmm", "vR2", "sSq3", "vVar3", "sRs3", "vOut",
            "cc", "vC", "vSP", "sC", "vBn2", "vMV2", "vBn3", "vMV3", "gU", "qwv",
        ]
        sem = {n: E(nc.semaphore(n)) for n in sem_names}

        QX = ["qx0", "qx1", "qx2s", "qx3", "qx4", "qx5"]
        QX2 = ["qxa", "qxb", "qxc", "qxd"]
        QST = ["qsta", "qstb"]
        U1ap = U1.ap()
        U1f = U1ap.bitcast(F32)
        nfm_f = nfm.ap().bitcast(F32)

        def xin_slot(j):
            s = j % 6
            return U1f[:, 8192 + s * 1024: 8192 + (s + 1) * 1024]

        def n_slot(j):
            s = j % 4
            return U1ap[:, 28672 + s * 1024: 28672 + (s + 1) * 1024]

        def y1bf_slot(j):
            s = j % 2
            return U1ap[:, 30720 + s * 1024: 30720 + (s + 1) * 1024]

        def x2_slot(j):
            s = j % 4
            return U1f[:, s * 1024: (s + 1) * 1024]

        def y1f_slot(j):
            s = j % 2
            return U1f[:, 4096 + s * 1024: 4096 + (s + 1) * 1024]

        def out_slot(j):
            s = j % 2
            return U1f[:, 6144 + s * 1024: 6144 + (s + 1) * 1024]
        stage = U1f[:, 0:8192]
        stage_v = U1f[:, 8192:16384]

        def ascratch(fc):
            s = fc % 4
            return nfm_f[:, s * 2048: (s + 1) * 2048]

        def kcol(ec, tg):
            return U1ap[:, ec * 2048 + tg * 512: ec * 2048 + (tg + 1) * 512]

        def vcol(ec, tg):
            return U1ap[:, 16384 + ec * 2048 + tg * 512: 16384 + ec * 2048 + (tg + 1) * 512]

        GAM = lambda c: smalls[:, c:c + 1]
        BET = lambda c: smalls[:, 8 + c: 9 + c]
        BK = lambda c: smalls[:, 16 + c: 17 + c]
        BV = lambda c: smalls[:, 32 + c: 33 + c]
        MASK = smalls[:, 40:104]
        EPSC = smalls[:, 104:105]

        HALVES = None

        txlog = []
        dbg_gate = {}

        with nc.Block() as block:
            HALVES = [lambda: pt0[:, 0:512], lambda: pt0[:, 512:1024],
                      lambda: pt1[:, 0:512], lambda: pt1[:, 512:1024]]
            pt0bf = pt0.ap().bitcast(BF16)
            pt1bf = pt1.ap().bitcast(BF16)
            HALVES_BF = [lambda: pt0bf[:, 0:512], lambda: pt0bf[:, 1024:1536],
                         lambda: pt1bf[:, 0:512], lambda: pt1bf[:, 1024:1536]]
            py1bf = py1.ap().bitcast(BF16)

            mev_after = []
            tot = 0
            for h in range(2):
                for p4 in range(4):
                    tot += (1 if h == 0 else 2)
                    mev_after.append(tot)

            @block.sync
            def _(sync):
                for j in range(NT):
                    if j >= 6:
                        sync.wait_ge(sem["vNorm"], j - 5)
                    sync.dma_start(out=xin_slot(j), in_=x_d[j * 128:(j + 1) * 128, :]
                                   ).then_inc(sem[QX[j % 6]], 16)
                sync.wait_ge(sem["qm1"], 96 * 16)
                sync.wait_ge(sem["sKAmm"], 32)
                for (srcd, colh, dc0) in [(wrt_d, 0, 0), (wit_d, 1, 0),
                                          (witn_d, 0, 4), (wrt_d, 1, 4)]:
                    src = srcd.ap().rearrange("(dc p) o -> p dc o", p=128)
                    dst = bass.AP(wkaT, dc0 * 1024 + colh * 512,
                                  [[8192, 128], [1024, 4], [1, 512]])
                    sync.dma_start(out=dst, in_=src).then_inc(sem["qm2"], 16)
                for j in range(NT):
                    if j < 4:
                        sync.wait_ge(sem["vPk"], j + 1)
                    else:
                        sync.wait_ge(sem["vBn2"], 2 * (j - 3))
                        sync.wait_ge(sem["vY1"], j - 3)
                    sync.dma_start(out=x2_slot(j), in_=x_d[j * 128:(j + 1) * 128, :]
                                   ).then_inc(sem[QX2[j % 4]], 16)
                sync.wait_ge(sem["qsta"], 16 * 8)
                sync.wait_ge(sem["qstb"], 16 * 8)

            @block.gpsimd
            def _(gpsimd):
                nqs = 0
                for dst, src in [
                    (smalls[:], smalls_d), (eye32[:], eye32_d), (eyebf[:], eyebf_d),
                    (onesbf[:], onesbf_d), (g1r[:], g1r_d), (g2r[:], g2r_d),
                    (g1i[:], g1i_d), (g2i[:], g2i_d), (bcat[:], bcat_d),
                ]:
                    gpsimd.dma_start(out=dst, in_=src).then_inc(sem["qs"], 16)
                    nqs += 16
                    gpsimd.wait_ge(sem["qs"], nqs)
                n_d1 = 0
                for h in range(2):
                    for p4 in range(4):
                        gidx = h * 4 + p4
                        gpsimd.wait_ge(sem["sMev"], mev_after[gidx])
                        slot = t4a if (p4 % 2 == 0) else t4b
                        for ti in range(2):
                            t = p4 * 2 + ti
                            srcs = [(slot, wrt_d)] if h == 0 else [(slot, wit_d), (t4n, witn_d)]
                            for (st_, dstd) in srcs:
                                for dpp in range(4):
                                    off = ((4 * t + dpp) * 16) * 512
                                    dst = bass.AP(dstd, off, [[16, 32], [512, 16], [1, 16]])
                                    src = st_[dpp * 32:(dpp + 1) * 32, ti * 256:(ti + 1) * 256]
                                    gpsimd.dma_start(out=dst, in_=src).then_inc(sem["qm1"], 16)
                                    n_d1 += 1
                        gpsimd.wait_ge(sem["qm1"], 16 * n_d1)
                        gpsimd.sem_inc(sem["qm1done"], 1)
                assert n_d1 == 96, n_d1
                gpsimd.wait_ge(sem["sKmm"], 32)
                for i in range(4):
                    gpsimd.dma_start(out=wkT.ap()[:, i * 2048:(i + 1) * 2048],
                                     in_=wv_d[:, i * 2048:(i + 1) * 2048]
                                     ).then_inc(sem["qwv"], 16)
                qg = [0]

                def tap(name, srcap, waitsem, waitval):
                    if name in dbg_out:
                        gpsimd.wait_ge(sem[waitsem], waitval)
                        gpsimd.dma_start(out=dbg_out[name], in_=srcap).then_inc(sem["qg"], 16)
                        qg[0] += 16
                        dbg_gate[name] = qg[0]
                        gpsimd.wait_ge(sem["qg"], qg[0])

                def gdma(dst, srcap):
                    gpsimd.dma_start(out=dst, in_=srcap).then_inc(sem["qg"], 16)
                    qg[0] += 16
                    gpsimd.wait_ge(sem["qg"], qg[0])

                tap("dbg_n", nfm.ap()[:], "sNev", 32)
                tap("dbg_k", U1ap[:, 0:16384], "sKev", 32)
                tap("dbg_a", abuf.ap()[:], "sAev", 32)
                tap("dbg_ta", sTA[:], "vTa", 1)
                gpsimd.wait_ge(sem["vTa"], 1)
                gdma(cca_in.ap(), sTA[:])
                gpsimd.collective_compute(
                    "AllGather", OP.bypass, ins=[cca_in.ap().opt()],
                    outs=[cca_out.ap().opt()], replica_groups=[list(range(NCORE))],
                ).then_inc(sem["cc"], 1)
                gpsimd.wait_ge(sem["cc"], 1)
                gdma(g_a[:], cca_out.ap().rearrange("(r p) f -> p r f", p=128))
                dbg_gate["__ga"] = qg[0]
                tap("dbg_v", U1ap[:, 16384:32768], "sVev", 32)
                tap("dbg_pa", sPa[:], "vPa", 1)
                gpsimd.wait_ge(sem["vTp"], 8)
                tap("dbg_pk", abuf.ap()[:], "vTp", 8)
                tap("dbg_tp", sTP[:], "vTp", 8)
                gdma(ccp_in.ap(), sTP[:])
                gpsimd.collective_compute(
                    "AllGather", OP.bypass, ins=[ccp_in.ap().opt()],
                    outs=[ccp_out.ap().opt()], replica_groups=[list(range(NCORE))],
                ).then_inc(sem["cc"], 1)
                gpsimd.wait_ge(sem["cc"], 2)
                gdma(g_p[:], ccp_out.ap().rearrange("(r p) f -> p r f", p=128))
                dbg_gate["__gp"] = qg[0]
                tap("dbg_pp", sPp[:], "vPp", 1)
                tap("dbg_o", abuf.ap()[:], "vO", 8)
                tap("dbg_m", wkaT.ap()[:], "qm2", 64)

            @block.scalar
            def _(scalar):
                scalar.dma_start(out=abuf.ap()[:, 0:8192], in_=wk_d
                                 ).then_inc(sem["qw"], 16)
                scalar.dma_start(out=abuf.ap()[:, 8192:16384], in_=wka_d
                                 ).then_inc(sem["sTx"], 16)
                scalar.wait_ge(sem["qs"], 16)
                sc = [0]
                def schain(ins):
                    sc[0] += 1
                    ins.then_inc(sem["sC"], 1)
                    scalar.wait_ge(sem["sC"], sc[0])
                def rs1(j):
                    scalar.wait_ge(sem["vMV"], j + 1)
                    schain(scalar.activation(slntmp[:, j:j + 1], smv[:, 2 * j + 1: 2 * j + 2],
                                      AF.Ln, bias=EPSC))
                    scalar.activation(srs1[:, j:j + 1], slntmp[:, j:j + 1],
                                      AF.Exp, scale=-0.5).then_inc(sem["sRs1"], 1)
                rs1(0)
                rs1(1)
                scalar.wait_ge(sem["qw"], 16)
                for i in range(16):
                    scalar.activation(wkT.ap()[:, i * 512:(i + 1) * 512],
                                      abuf.ap()[:, i * 512:(i + 1) * 512],
                                      AF.Identity, scale=1.0).then_inc(sem["sWev"], 1)
                    if i == 7:
                        rs1(2)
                        rs1(3)
                def emit_nev(g):
                    for fc in range(8):
                        k = 32 + g * 8 + fc
                        scalar.wait_ge(sem["sNT"], g * 8 + fc + 1)
                        scalar.activation(
                            nfm.ap()[:, fc * 2048 + g * 512: fc * 2048 + (g + 1) * 512],
                            HALVES_BF[k % 4](), AF.Identity, scale=GAM(fc), bias=BET(fc)).then_inc(sem["sNev"], 1)

                def emit_Kev(tg):
                    rs1_at = {(0, 0): 4, (0, 1): 5, (0, 3): 6, (0, 5): 7, (0, 7): 8,
                              (1, 0): 9, (1, 2): 10, (1, 4): 11,
                              (2, 0): 12, (2, 2): 13, (2, 4): 14, (2, 6): 15}
                    for ec in range(8):
                        if (tg, ec) in rs1_at:
                            rs1(rs1_at[(tg, ec)])
                        gi = tg * 8 + ec
                        scalar.wait_ge(sem["sKmm"], gi + 1)
                        ps = pf[:, 0:512] if gi % 2 == 0 else pf[:, 512:1024]
                        scalar.activation(kcol(ec, tg), ps, AF.Identity,
                                          scale=1.0, bias=BK(ec)).then_inc(sem["sKev"], 1)
                        if gi % 4 == 3:
                            gidx = gi // 4
                            h, p4 = divmod(gidx, 4)
                            scalar.wait_ge(sem["sM1"], gidx + 1)
                            if gidx >= 1:
                                scalar.wait_ge(sem["qm1done"], gidx)
                            dstslot = t4a if (p4 % 2 == 0) else t4b
                            ins_ = scalar.activation(dstslot[:], pm[:], AF.Identity, scale=1.0)
                            if h == 1:
                                scalar.activation(t4n[:], pm[:], AF.Identity, scale=-1.0
                                                  ).then_inc(sem["sMev"], 2)
                            else:
                                ins_.then_inc(sem["sMev"], 1)

                def emit_expev(tg):
                    if tg == 0:
                        scalar.wait_ge(sem["vInit"], 1)
                        scalar.wait_ge(sem["sWev"], 32)
                    for fc in range(8):
                        gi = tg * 8 + fc
                        scalar.wait_ge(sem["sKAmm"], gi + 1)
                        ps = pf[:, 0:512] if gi % 2 == 0 else pf[:, 512:1024]
                        scalar.activation(
                            abuf.ap()[:, fc * 2048 + tg * 512: fc * 2048 + (tg + 1) * 512],
                            ps, AF.Exp, scale=1.0 / 32.0, bias=bka32[:, fc:fc + 1],
                            accum_out=expacc[:, fc * 4 + tg: fc * 4 + tg + 1]).then_inc(sem["sAev"], 1)

                for tg in range(4):
                    emit_nev(tg)
                    if tg == 1:
                        scalar.wait_ge(sem["sTx"], 16)
                        for i in range(16):
                            scalar.activation(wkaT.ap()[:, i * 512:(i + 1) * 512],
                                              abuf.ap()[:, 8192 + i * 512: 8192 + (i + 1) * 512],
                                              AF.Identity, scale=1.0).then_inc(sem["sWev"], 1)
                    emit_Kev(tg)
                    if tg >= 1:
                        emit_expev(tg - 1)
                emit_expev(3)
                scalar.wait_ge(sem["vNorm"], 16)
                scalar.wait_ge(sem["sNT"], 32)
                for gi in range(32):
                    tg, ec = divmod(gi, 8)
                    scalar.wait_ge(sem["sVmm"], gi + 1)
                    ps = pf[:, 0:512] if gi % 2 == 0 else pf[:, 512:1024]
                    scalar.activation(vcol(ec, tg), ps, AF.Identity,
                                      scale=1.0, bias=BV(ec)).then_inc(sem["sVev"], 1)
                for fc in range(8):
                    scalar.wait_ge(sem["vScanA"], fc + 1)
                    if fc == 0:
                        scalar.wait_ge(sem["vPa"], 1)
                    schain(scalar.activation(ascratch(fc), ascratch(fc), AF.Ln,
                                             bias=sPa[:, fc:fc + 1]))
                    scalar.activation(ascratch(fc), ascratch(fc), AF.Exp, scale=-1.0).then_inc(sem["sRA"], 1)
                for j in range(NT + 2):
                    if 1 <= j <= NT:
                        jj = j - 1
                        for hh in range(2):
                            scalar.wait_ge(sem["sY1T"], 2 * jj + hh + 1)
                            dst = bass.AP(nfm, (4 * hh) * 2048 + jj * 128,
                                          [[NC * T, 128], [2048, 4], [1, 128]])
                            srcb = py1 if hh == 0 else pm
                            scalar.activation(dst, srcb[:, 0:512], AF.Identity, scale=1.0).then_inc(sem["sY1ev"], 1)
                    if j < NT:
                        scalar.wait_ge(sem["vMV2"], j + 1)
                        schain(scalar.activation(slntmp[:, j:j + 1], smv2[:, 2 * j + 1: 2 * j + 2],
                                          AF.Ln, bias=EPSC))
                        scalar.activation(srs2[:, j:j + 1], slntmp[:, j:j + 1],
                                          AF.Exp, scale=-0.5).then_inc(sem["sRs2"], 1)
                    if j >= 2:
                        jj = j - 2
                        scalar.wait_ge(sem["vMV3"], jj + 1)
                        schain(scalar.activation(slntmp[:, jj:jj + 1], smv3[:, 2 * jj + 1: 2 * jj + 2],
                                          AF.Ln, bias=EPSC))
                        scalar.activation(srs3[:, jj:jj + 1], slntmp[:, jj:jj + 1],
                                          AF.Exp, scale=-0.5).then_inc(sem["sRs3"], 1)
                    if j >= 3:
                        jj = j - 3
                        scalar.wait_ge(sem["vOut"], jj + 1)
                        scalar.dma_start(out=out_d[jj * 128:(jj + 1) * 128, :], in_=out_slot(jj)
                                         ).then_inc(sem[QST[jj % 2]], 16)
                for jj in range(NT - 1, NT):
                    scalar.wait_ge(sem["vOut"], jj + 1)
                    scalar.dma_start(out=out_d[jj * 128:(jj + 1) * 128, :], in_=out_slot(jj)
                                     ).then_inc(sem[QST[jj % 2]], 16)

            @block.tensor
            def _(tensor):
                def tx_wait(k):
                    if k >= 4:
                        s, v = txlog[k - 4]
                        tensor.wait_ge(sem[s], v)

                tensor.wait_ge(sem["qs"], 48)
                evlog = []

                def pp_wait(g_global):
                    if g_global >= 2:
                        sname, v = evlog[g_global - 2]
                        tensor.wait_ge(sem[sname], v)

                nproj = [0]

                def emit_nT(g):
                    tensor.wait_ge(sem["vNorm"], 4 * (g + 1))
                    for fc in range(8):
                        k = g * 8 + fc
                        tx_wait(k)
                        half = HALVES_BF[k % 4]()
                        for i in range(4):
                            j = 4 * g + i
                            ins_ = tensor.transpose(half[:, i * 128:(i + 1) * 128],
                                             n_slot(j)[:, fc * 128:(fc + 1) * 128],
                                             eyebf[:])
                            if i == 3:
                                ins_.then_inc(sem["sNT"], 1)
                        txlog.append(("sNev", g * 8 + fc + 1))

                def emit_K(tg):
                    if tg == 0:
                        tensor.wait_ge(sem["sWev"], 16)
                    for ec in range(8):
                        gi = tg * 8 + ec
                        tensor.wait_ge(sem["sNev"], 8 * (tg + 1))
                        pp_wait(nproj[0])
                        nproj[0] += 1
                        ps = pf[:, 0:512] if gi % 2 == 0 else pf[:, 512:1024]
                        for dc in range(8):
                            ins_ = tensor.matmul(ps,
                                          wkT.ap()[:, dc * 1024 + ec * 128: dc * 1024 + (ec + 1) * 128],
                                          nfm.ap()[:, dc * 2048 + tg * 512: dc * 2048 + (tg + 1) * 512],
                                          start=(dc == 0), stop=(dc == 7))
                            if dc == 7:
                                ins_.then_inc(sem["sKmm"], 1)
                        evlog.append(("sKev", gi + 1))
                        if gi % 4 == 3:
                            gidx = gi // 4
                            if gidx == 0:
                                tensor.wait_ge(sem["qs"], 16 * 9)
                            if gidx > 0:
                                tensor.wait_ge(sem["sMev"], mev_after[gidx - 1])
                            h, p4 = divmod(gidx, 4)
                            sg1, sg2 = (g1r, g2r) if h == 0 else (g1i, g2i)
                            for ti in range(2):
                                t = p4 * 2 + ti
                                ins_ = tensor.matmul(pm[:, ti * 256:(ti + 1) * 256],
                                              sg1[0:4, t * 128:(t + 1) * 128], sg2[0:4, :],
                                              start=True, stop=True)
                                if ti == 1:
                                    ins_.then_inc(sem["sM1"], 1)

                def emit_KA(tg):
                    if tg == 0:
                        tensor.wait_ge(sem["sWev"], 32)
                    for fc in range(8):
                        gi = tg * 8 + fc
                        tensor.wait_ge(sem["sKev"], 8 * (tg + 1))
                        pp_wait(nproj[0])
                        nproj[0] += 1
                        ps = pf[:, 0:512] if gi % 2 == 0 else pf[:, 512:1024]
                        for ec in range(8):
                            ins_ = tensor.matmul(ps,
                                          wkaT.ap()[:, ec * 1024 + fc * 128: ec * 1024 + (fc + 1) * 128],
                                          kcol(ec, tg),
                                          start=(ec == 0), stop=(ec == 7))
                            if ec == 7:
                                ins_.then_inc(sem["sKAmm"], 1)
                        evlog.append(("sAev", gi + 1))

                for tg in range(4):
                    emit_nT(tg)
                    emit_K(tg)
                    if tg >= 1:
                        emit_KA(tg - 1)
                emit_KA(3)
                tensor.wait_ge(sem["qwv"], 64)
                for gi in range(32):
                    tg, ec = divmod(gi, 8)
                    pp_wait(64 + gi)
                    ps = pf[:, 0:512] if gi % 2 == 0 else pf[:, 512:1024]
                    for dc in range(8):
                        ins_ = tensor.matmul(ps,
                                      wkT.ap()[:, dc * 1024 + ec * 128: dc * 1024 + (ec + 1) * 128],
                                      nfm.ap()[:, dc * 2048 + tg * 512: dc * 2048 + (tg + 1) * 512],
                                      start=(dc == 0), stop=(dc == 7))
                        if dc == 7:
                            ins_.then_inc(sem["sVmm"], 1)
                    evlog.append(("sVev", gi + 1))
                tensor.wait_ge(sem["sMev"], mev_after[-1])
                tensor.wait_ge(sem["qm2"], 64)
                for lad, tot in [("vScanA", 8), ("sRA", 8), ("vPk", 8), ("vO", 7)]:
                    for i in range(1, tot + 1):
                        tensor.wait_ge(sem[lad], i)
                        for _ in range(3):
                            tensor.matmul(pm[:], eyebf[:], wkaT.ap()[:, 0:512],
                                          start=True, stop=True)
                def emit_oT(k):
                    if k > 0:
                        tensor.wait_ge(sem["vR1"], k)
                    for fc in range(8):
                        if k == 0:
                            tensor.wait_ge(sem["vO"], fc + 1)
                        ins_ = tensor.transpose(pt0[:, fc * 128:(fc + 1) * 128],
                                         abuf.ap()[:, fc * 2048 + k * 128: fc * 2048 + k * 128 + 128],
                                         eye32[:])
                        if fc == 7:
                            ins_.then_inc(sem["sOT"], 1)

                def emit_y1T(j):
                    tensor.wait_ge(sem["vY1"], j + 1)
                    for hh in range(2):
                        if 2 * j + hh >= 2:
                            tensor.wait_ge(sem["sY1ev"], 2 * j + hh - 1)
                        bank = py1 if hh == 0 else pm
                        for i in range(4):
                            dcc = hh * 4 + i
                            ins_ = tensor.transpose(bank[:, i * 128:(i + 1) * 128],
                                             y1f_slot(j)[:, dcc * 128:(dcc + 1) * 128],
                                             eye32[:])
                            if i == 3:
                                ins_.then_inc(sem["sY1T"], 1)

                def emit_FFN(j):
                    tensor.wait_ge(sem["sY1ev"], 2 * j + 1)
                    if j == 0:
                        tensor.wait_ge(sem["sVev"], 32)
                        tensor.wait_ge(sem["qm2"], 64)
                        tensor.wait_ge(sem["qs"], 16 * 9)
                    elif j >= 2:
                        tensor.wait_ge(sem["vR2"], j - 1)
                    pfj = pf if j % 2 == 0 else pt1
                    for oc in range(2):
                        for dcc in range(8):
                            if oc == 0 and dcc == 4:
                                tensor.wait_ge(sem["sY1ev"], 2 * (j + 1))
                            tensor.matmul(pfj[:, oc * 512:(oc + 1) * 512],
                                          nfm.ap()[:, dcc * 2048 + j * 128: dcc * 2048 + j * 128 + 128],
                                          wkaT.ap()[:, dcc * 1024 + oc * 512: dcc * 1024 + (oc + 1) * 512],
                                          start=(dcc == 0), stop=False)
                        ins_ = tensor.matmul(pfj[:, oc * 512:(oc + 1) * 512],
                                      onesbf[0:1, :], bcat[0:1, oc * 512:(oc + 1) * 512],
                                      start=False, stop=True)
                        if oc == 1:
                            ins_.then_inc(sem["sFmm"], 1)

                emit_oT(0)
                for j in range(NT):
                    if j + 1 < NT:
                        emit_oT(j + 1)
                    emit_y1T(j)
                    emit_FFN(j)
                    for lad, v in [("vOut", j - 1), ("vR1", j + 1), ("vMV2", j + 1), ("vY1", j + 1)]:
                        if 1 <= v <= NT:
                            tensor.wait_ge(sem[lad], v)
                            for _ in range(3):
                                tensor.ldweights(eyebf[:])

            @block.vector
            def _(vector):
                vc = [0]
                def vchain(ins):
                    vc[0] += 1
                    ins.then_inc(sem["vC"], 1)
                    vector.wait_ge(sem["vC"], vc[0])
                vector.wait_ge(sem["qs"], 16)
                vector.tensor_scalar(bka32[:], smalls[:, 24:32], 1.0 / 32.0, None,
                                     op0=OP.mult).then_inc(sem["vInit"], 1)
                for j in range(NT):
                    vector.wait_ge(sem[QX[j % 6]], 16 * (j // 6 + 1))
                    vector.bn_stats(bnst[:, 0:6], xin_slot(j)[:, 0:512]).then_inc(sem["vBn"], 1)
                    vector.bn_stats(bnst[:, 6:12], xin_slot(j)[:, 512:1024]).then_inc(sem["vBn"], 1)
                    vector.wait_ge(sem["vBn"], 2 * (j + 1))
                    vector.bn_aggr(smv[:, 2 * j: 2 * j + 2], bnst[:]).then_inc(sem["vMV"], 1)
                    vector.wait_ge(sem["sRs1"], j + 1)
                    if j >= 4:
                        vector.wait_ge(sem["sNT"], 8 * (j // 4))
                    vector.tensor_scalar(n_slot(j), xin_slot(j),
                                         smv[:, 2 * j: 2 * j + 1], srs1[:, j:j + 1],
                                         op0=OP.subtract, op1=OP.mult).then_inc(sem["vNorm"], 1)
                vector.wait_ge(sem["sAev"], 32)
                vector.tensor_reduce(sTA[:], expacc[:].rearrange("p (f g) -> p f g", f=8),
                                     axis=AX.X, op=OP.add).then_inc(sem["vTa"], 1)
                vector.wait_ge(sem["qg"], dbg_gate["__ga"])
                vchain(vector.tensor_tensor(g_m[:], g_a[:], MASK, op=OP.mult))
                vector.tensor_reduce(sPa[:], g_m[:].rearrange("p (r f) -> p f r", r=8),
                                     axis=AX.X, op=OP.add).then_inc(sem["vPa"], 1)

                def aslice(fc):
                    return abuf.ap()[:, fc * 2048:(fc + 1) * 2048]

                def scan_a(fc):
                    if fc == 0:
                        vector.wait_ge(sem["sVmm"], 32)
                    vector.tensor_tensor_scan(ascratch(fc), aslice(fc), aslice(fc),
                                              0.0, op0=OP.add, op1=OP.bypass
                                              ).then_inc(sem["vScanA"], 1)

                def u_pk(fc):
                    if fc == 0 and "dbg_a" in dbg_gate:
                        vector.wait_ge(sem["qg"], dbg_gate["dbg_a"])
                    vector.wait_ge(sem["sRA"], fc + 1)
                    vchain(vector.tensor_tensor(aslice(fc), aslice(fc), ascratch(fc), op=OP.mult))
                    vchain(vector.tensor_tensor(aslice(fc), aslice(fc),
                                                U1ap[:, fc * 2048:(fc + 1) * 2048],
                                                op=OP.mult))
                    vector.tensor_tensor_scan(aslice(fc), aslice(fc), aslice(fc),
                                              0.0, op0=OP.add, op1=OP.bypass
                                              ).then_inc(sem["vPk"], 1)
                    vector.wait_ge(sem["vPk"], fc + 1)
                    vector.tensor_copy(sTP[:, fc:fc + 1],
                                       abuf.ap()[:, fc * 2048 + 2047: fc * 2048 + 2048]
                                       ).then_inc(sem["vTp"], 1)

                def o1(fc):
                    if fc == 0:
                        vector.wait_ge(sem["sVev"], 32)
                        if "dbg_pk" in dbg_gate:
                            vector.wait_ge(sem["qg"], dbg_gate["dbg_pk"])
                    vector.wait_ge(sem["vTp"], fc + 1)
                    vector.tensor_tensor(aslice(fc), aslice(fc),
                                         U1ap[:, 16384 + fc * 2048: 16384 + (fc + 1) * 2048],
                                         op=OP.mult).then_inc(sem["vSP"], 1)

                scan_a(0)
                for fc in range(1, 8):
                    scan_a(fc)
                    u_pk(fc - 1)
                u_pk(7)
                for fc in range(8):
                    o1(fc)
                vector.wait_ge(sem["qg"], dbg_gate["__gp"])
                vchain(vector.tensor_tensor(g_m[:], g_p[:], MASK, op=OP.mult))
                vector.tensor_reduce(sPp[:], g_m[:].rearrange("p (r f) -> p f r", r=8),
                                     axis=AX.X, op=OP.add).then_inc(sem["vPp"], 1)
                vector.wait_ge(sem["vPp"], 1)
                for fc in range(8):
                    vector.wait_ge(sem["vSP"], fc + 1)
                    vector.scalar_tensor_tensor(aslice(fc),
                                                U1ap[:, 16384 + fc * 2048: 16384 + (fc + 1) * 2048],
                                                sPp[:, fc:fc + 1], aslice(fc),
                                                op0=OP.mult, op1=OP.add).then_inc(sem["vO"], 1)
                for j in range(NT + 3):
                    if j >= 3:
                        jj = j - 3
                        vector.wait_ge(sem["sRs3"], jj + 1)
                        if jj >= 2:
                            vector.wait_ge(sem[QST[jj % 2]], 16 * (jj // 2))
                        vector.tensor_scalar(out_slot(jj), y1f_slot(jj),
                                             smv3[:, 2 * jj: 2 * jj + 1], srs3[:, jj:jj + 1],
                                             op0=OP.subtract, op1=OP.mult).then_inc(sem["vOut"], 1)
                    if 1 <= j <= NT:
                        jj = j - 1
                        vector.wait_ge(sem["sRs2"], jj + 1)
                        if jj >= 2:
                            vector.wait_ge(sem["vOut"], jj - 1)
                        vector.tensor_scalar(y1f_slot(jj), x2_slot(jj),
                                             smv2[:, 2 * jj: 2 * jj + 1], srs2[:, jj:jj + 1],
                                             op0=OP.subtract, op1=OP.mult).then_inc(sem["vY1"], 1)
                    if j < NT:
                        vector.wait_ge(sem["sOT"], j + 1)
                        vector.wait_ge(sem[QX2[j % 4]], 16 * (j // 4 + 1))
                        vector.scalar_tensor_tensor(x2_slot(j), pt0[:, 0:1024], 0.0, x2_slot(j),
                                                    op0=OP.add, op1=OP.add).then_inc(sem["vR1"], 1)
                        vector.wait_ge(sem["vR1"], j + 1)
                        vector.bn_stats(bnst2[:, 0:6], x2_slot(j)[:, 0:512]).then_inc(sem["vBn2"], 1)
                        vector.bn_stats(bnst2[:, 6:12], x2_slot(j)[:, 512:1024]).then_inc(sem["vBn2"], 1)
                        vector.wait_ge(sem["vBn2"], 2 * (j + 1))
                        vector.bn_aggr(smv2[:, 2 * j: 2 * j + 2], bnst2[:]).then_inc(sem["vMV2"], 1)
                    if 2 <= j < NT + 2:
                        jj = j - 2
                        pfj = pf if jj % 2 == 0 else pt1
                        vector.wait_ge(sem["sFmm"], jj + 1)
                        vector.scalar_tensor_tensor(y1f_slot(jj), pfj[:, 0:1024], 0.0, y1f_slot(jj),
                                                    op0=OP.add, op1=OP.add).then_inc(sem["vR2"], 1)
                        vector.wait_ge(sem["vR2"], jj + 1)
                        vector.bn_stats(bnst3[:, 0:6], y1f_slot(jj)[:, 0:512]).then_inc(sem["vBn3"], 1)
                        vector.bn_stats(bnst3[:, 6:12], y1f_slot(jj)[:, 512:1024]).then_inc(sem["vBn3"], 1)
                        vector.wait_ge(sem["vBn3"], 2 * (jj + 1))
                        vector.bn_aggr(smv3[:, 2 * jj: 2 * jj + 2], bnst3[:]).then_inc(sem["vMV3"], 1)

    return nc


def kernel(L0_attn_hidden_states, L0_attn_attention_mask, L0_attn_Wk, L0_attn_bk,
           L0_attn_Wka, L0_attn_bka, L0_attn_Wv, L0_attn_bv, L0_attn_gamma,
           L0_attn_beta, L0_ffn_G1_R, L0_ffn_G2_R, L0_ffn_G1_I, L0_ffn_G2_I,
           L0_ffn_bias_real, L0_ffn_bias_imag):
    x = np.ascontiguousarray(
        np.asarray(L0_attn_hidden_states, dtype=np.float32).reshape(NCORE * T, D))
    def wprep(w):
        return np.ascontiguousarray(
            np.asarray(w, dtype=np.float32).T.reshape(8, 128, 1024)
            .transpose(1, 0, 2).reshape(128, 8192))
    wk = wprep(L0_attn_Wk)
    wka = wprep(L0_attn_Wka)
    wv = wprep(L0_attn_Wv)

    def col8(v):
        return np.ascontiguousarray(np.asarray(v, dtype=np.float32).reshape(8, 128).T)

    smalls = np.zeros((128, 112), dtype=np.float32)
    smalls[:, 104] = EPS
    smalls[:, 0:8] = col8(L0_attn_gamma)
    smalls[:, 8:16] = col8(L0_attn_beta)
    smalls[:, 16:24] = col8(L0_attn_bk)
    smalls[:, 24:32] = col8(L0_attn_bka)
    smalls[:, 32:40] = col8(L0_attn_bv)

    g1r = np.ascontiguousarray(
        np.asarray(L0_ffn_G1_R, np.float32)[0].transpose(1, 2, 0).reshape(4, 1024))
    g2r = np.ascontiguousarray(
        np.asarray(L0_ffn_G2_R, np.float32)[:, :, 0, :].transpose(0, 2, 1).reshape(4, 256))
    g1i = np.ascontiguousarray(
        np.asarray(L0_ffn_G1_I, np.float32)[0].transpose(1, 2, 0).reshape(4, 1024))
    g2i = np.ascontiguousarray(
        np.asarray(L0_ffn_G2_I, np.float32)[:, :, 0, :].transpose(0, 2, 1).reshape(4, 256))
    bcat = np.ascontiguousarray(np.concatenate([
        np.asarray(L0_ffn_bias_real, np.float32),
        np.asarray(L0_ffn_bias_imag, np.float32)]).reshape(1, 1024))

    eye32 = np.eye(128, dtype=np.float32)
    eyebf = np.eye(128).astype(ml_dtypes.bfloat16)
    onesbf = np.ones((1, 128), dtype=np.float32).astype(ml_dtypes.bfloat16)

    nc = build_nc()
    in_maps = []
    for c in range(NCORE):
        m = np.zeros((8, 8), dtype=np.float32)
        m[:c, :] = 1.0
        sm = smalls.copy()
        sm[:, 40:104] = np.broadcast_to(m.reshape(1, 64), (128, 64))
        in_maps.append({
            "x": np.ascontiguousarray(x[c * T:(c + 1) * T]),
            "wk": wk, "wka": wka, "wv": wv,
            "smalls": sm, "eye32": eye32, "eyebf": eyebf, "onesbf": onesbf,
            "g1r": g1r, "g2r": g2r, "g1i": g1i, "g2i": g2i, "bcat": bcat,
        })
    res = run_bass_kernel_spmd(nc, in_maps, core_ids=list(range(NCORE)))
    out = np.concatenate([np.asarray(r["out"]) for r in res.results], axis=0)
    return out.reshape(1, NCORE * T, D)


# version 35
# speedup vs baseline: 1.0099x; 1.0099x over previous
import numpy as np
import ml_dtypes
import concourse.bass as bass
import concourse.mybir as mybir
from concourse.alu_op_type import AluOpType
from concourse.bass_utils import run_bass_kernel_spmd
from contextlib import ExitStack

F32 = mybir.dt.float32
BF16 = mybir.dt.bfloat16
AF = mybir.ActivationFunctionType
AX = mybir.AxisListType
OP = AluOpType

NCORE = 8
T = 2048
D = 1024
NT = 16
NC = 8
TG = 512
EPS = 1e-5


def build_nc(dbg=()):
    nc = bass.Bass("TRN2", target_bir_lowering=False, debug=False)

    x_d = nc.dram_tensor("x", [T, D], F32, kind="ExternalInput").ap()
    wk_d = nc.dram_tensor("wk", [128, 8192], F32, kind="ExternalInput").ap()
    wka_d = nc.dram_tensor("wka", [128, 8192], F32, kind="ExternalInput").ap()
    wv_d = nc.dram_tensor("wv", [128, 8192], F32, kind="ExternalInput").ap()
    smalls_d = nc.dram_tensor("smalls", [128, 112], F32, kind="ExternalInput").ap()
    eye32_d = nc.dram_tensor("eye32", [128, 128], F32, kind="ExternalInput").ap()
    eyebf_d = nc.dram_tensor("eyebf", [128, 128], BF16, kind="ExternalInput").ap()
    onesbf_d = nc.dram_tensor("onesbf", [1, 128], BF16, kind="ExternalInput").ap()
    g1r_d = nc.dram_tensor("g1r", [4, 1024], F32, kind="ExternalInput").ap()
    g2r_d = nc.dram_tensor("g2r", [4, 256], F32, kind="ExternalInput").ap()
    g1i_d = nc.dram_tensor("g1i", [4, 1024], F32, kind="ExternalInput").ap()
    g2i_d = nc.dram_tensor("g2i", [4, 256], F32, kind="ExternalInput").ap()
    bcat_d = nc.dram_tensor("bcat", [1, 1024], F32, kind="ExternalInput").ap()
    out_d = nc.dram_tensor("out", [T, D], F32, kind="ExternalOutput").ap()

    wrt_d = nc.dram_tensor("wrt_s", [512, 512], BF16)
    wit_d = nc.dram_tensor("wit_s", [512, 512], BF16)
    witn_d = nc.dram_tensor("witn_s", [512, 512], BF16)
    cca_in = nc.dram_tensor("cca_in", [128, 8], F32)
    cca_out = nc.dram_tensor("cca_out", [1024, 8], F32)
    ccp_in = nc.dram_tensor("ccp_in", [128, 8], F32)
    ccp_out = nc.dram_tensor("ccp_out", [1024, 8], F32)

    dbg_out = {}
    for name, shape in [
        ("dbg_a", [128, NC * T]), ("dbg_k", [128, NC * T]), ("dbg_v", [128, NC * T]),
        ("dbg_ta", [128, 8]), ("dbg_pa", [128, 8]), ("dbg_pk", [128, NC * T]),
        ("dbg_o", [128, NC * T]), ("dbg_n", [128, NC * T]), ("dbg_m", [128, 8192]),
        ("dbg_tp", [128, 8]), ("dbg_pp", [128, 8]),
    ]:
        if name in dbg:
            dt = BF16 if name in ("dbg_k", "dbg_v", "dbg_n", "dbg_m") else F32
            dbg_out[name] = nc.dram_tensor(name, shape, dt, kind="ExternalOutput").ap()

    with ExitStack() as ctx:
        E = ctx.enter_context
        U1 = E(nc.sbuf_tensor("U1", [128, 32768], BF16))
        abuf = E(nc.sbuf_tensor("abuf", [128, NC * T], F32))
        nfm = E(nc.sbuf_tensor("nfm", [128, NC * T], BF16))
        wkT = E(nc.sbuf_tensor("wkT", [128, 8192], BF16))
        wkaT = E(nc.sbuf_tensor("wkaT", [128, 8192], BF16))
        smalls = E(nc.sbuf_tensor("smalls_s", [128, 112], F32))
        eye32 = E(nc.sbuf_tensor("eye32_s", [128, 128], F32))
        eyebf = E(nc.sbuf_tensor("eyebf_s", [128, 128], BF16))
        onesbf = E(nc.sbuf_tensor("onesbf_s", [1, 128], BF16))
        g1r = E(nc.sbuf_tensor("g1r_s", [4, 1024], BF16))
        g2r = E(nc.sbuf_tensor("g2r_s", [4, 256], BF16))
        g1i = E(nc.sbuf_tensor("g1i_s", [4, 1024], BF16))
        g2i = E(nc.sbuf_tensor("g2i_s", [4, 256], BF16))
        bcat = E(nc.sbuf_tensor("bcat_s", [1, 1024], BF16))
        t4a = E(nc.sbuf_tensor("t4a", [128, 512], BF16))
        t4b = E(nc.sbuf_tensor("t4b", [128, 512], BF16))
        t4n = E(nc.sbuf_tensor("t4n", [128, 512], BF16))
        bka32 = E(nc.sbuf_tensor("bka32", [128, 8], F32))
        bnst = E(nc.sbuf_tensor("bnst", [128, 12], F32))
        smv = E(nc.sbuf_tensor("smv", [128, 32], F32))
        srs1 = E(nc.sbuf_tensor("srs1", [128, 16], F32))
        slntmp = E(nc.sbuf_tensor("slntmp", [128, 16], F32))
        expacc = E(nc.sbuf_tensor("expacc", [128, 32], F32))
        sTA = E(nc.sbuf_tensor("sTA", [128, 8], F32))
        sTP = E(nc.sbuf_tensor("sTP", [128, 8], F32))
        sPa = E(nc.sbuf_tensor("sPa", [128, 8], F32))
        sPp = E(nc.sbuf_tensor("sPp", [128, 8], F32))
        g_a = E(nc.sbuf_tensor("g_a", [128, 64], F32))
        g_p = E(nc.sbuf_tensor("g_p", [128, 64], F32))
        g_m = E(nc.sbuf_tensor("g_m", [128, 64], F32))
        smv2 = E(nc.sbuf_tensor("smv2", [128, 32], F32))
        srs2 = E(nc.sbuf_tensor("srs2", [128, 16], F32))
        smv3 = E(nc.sbuf_tensor("smv3", [128, 32], F32))
        srs3 = E(nc.sbuf_tensor("srs3", [128, 16], F32))
        bnst2 = E(nc.sbuf_tensor("bnst2", [128, 12], F32))
        bnst3 = E(nc.sbuf_tensor("bnst3", [128, 12], F32))

        pm = E(nc.psum_tensor("pm", [128, 512], F32))
        pt0 = E(nc.psum_tensor("pt0", [128, 1024], F32))
        pt1 = E(nc.psum_tensor("pt1", [128, 1024], F32))
        pf = E(nc.psum_tensor("pf", [128, 1024], F32))
        py1 = E(nc.psum_tensor("py1", [128, 512], F32))

        sem_names = [
            "qx0", "qx1", "qx2s", "qx3", "qx4", "qx5",
            "qw", "qs", "qm1", "qm1done", "qm2",
            "qxa", "qxb", "qxc", "qxd", "qsta", "qstb", "qg",
            "sM1", "sMev", "sTx", "sWev", "sNT", "sNev",
            "sKmm", "sKev", "sVmm", "sVev", "sKAmm", "sAev",
            "vBn", "vMV", "sRs1", "vNorm", "vInit",
            "vTa", "vPa", "vScanA", "sRA", "vPk", "vTp", "vPp", "vO",
            "sOT", "vR1", "sSq2", "vVar2", "sRs2", "vY1", "sY1bf",
            "sY1T", "sY1ev", "sFmm", "vR2", "sSq3", "vVar3", "sRs3", "vOut",
            "cc", "vC", "vSP", "sC", "vBn2", "vMV2", "vBn3", "vMV3", "gU", "qwv",
        ]
        sem = {n: E(nc.semaphore(n)) for n in sem_names}

        QX = ["qx0", "qx1", "qx2s", "qx3", "qx4", "qx5"]
        QX2 = ["qxa", "qxb", "qxc", "qxd"]
        QST = ["qsta", "qstb"]
        U1ap = U1.ap()
        U1f = U1ap.bitcast(F32)
        nfm_f = nfm.ap().bitcast(F32)

        def xin_slot(j):
            s = j % 6
            return U1f[:, 8192 + s * 1024: 8192 + (s + 1) * 1024]

        def n_slot(j):
            s = j % 4
            return U1ap[:, 28672 + s * 1024: 28672 + (s + 1) * 1024]

        def y1bf_slot(j):
            s = j % 2
            return U1ap[:, 30720 + s * 1024: 30720 + (s + 1) * 1024]

        def x2_slot(j):
            s = j % 4
            return U1f[:, s * 1024: (s + 1) * 1024]

        def y1f_slot(j):
            s = j % 2
            return U1f[:, 4096 + s * 1024: 4096 + (s + 1) * 1024]

        def out_slot(j):
            s = j % 2
            return U1f[:, 6144 + s * 1024: 6144 + (s + 1) * 1024]
        stage = U1f[:, 0:8192]
        stage_v = U1f[:, 8192:16384]

        def ascratch(fc):
            s = fc % 4
            return nfm_f[:, s * 2048: (s + 1) * 2048]

        def kcol(ec, tg):
            return U1ap[:, ec * 2048 + tg * 512: ec * 2048 + (tg + 1) * 512]

        def vcol(ec, tg):
            return U1ap[:, 16384 + ec * 2048 + tg * 512: 16384 + ec * 2048 + (tg + 1) * 512]

        GAM = lambda c: smalls[:, c:c + 1]
        BET = lambda c: smalls[:, 8 + c: 9 + c]
        BK = lambda c: smalls[:, 16 + c: 17 + c]
        BV = lambda c: smalls[:, 32 + c: 33 + c]
        MASK = smalls[:, 40:104]
        EPSC = smalls[:, 104:105]

        HALVES = None

        txlog = []
        dbg_gate = {}

        with nc.Block() as block:
            HALVES = [lambda: pt0[:, 0:512], lambda: pt0[:, 512:1024],
                      lambda: pt1[:, 0:512], lambda: pt1[:, 512:1024]]
            pt0bf = pt0.ap().bitcast(BF16)
            pt1bf = pt1.ap().bitcast(BF16)
            HALVES_BF = [lambda: pt0bf[:, 0:512], lambda: pt0bf[:, 1024:1536],
                         lambda: pt1bf[:, 0:512], lambda: pt1bf[:, 1024:1536]]
            py1bf = py1.ap().bitcast(BF16)

            mev_after = []
            tot = 0
            for h in range(2):
                for p4 in range(4):
                    tot += (1 if h == 0 else 2)
                    mev_after.append(tot)

            @block.sync
            def _(sync):
                for j in range(NT):
                    if j >= 6:
                        sync.wait_ge(sem["vNorm"], j - 5)
                    sync.dma_start(out=xin_slot(j), in_=x_d[j * 128:(j + 1) * 128, :]
                                   ).then_inc(sem[QX[j % 6]], 16)
                sync.wait_ge(sem["qm1"], 96 * 16)
                sync.wait_ge(sem["sKAmm"], 32)
                for (srcd, colh, dc0) in [(wrt_d, 0, 0), (wit_d, 1, 0),
                                          (witn_d, 0, 4), (wrt_d, 1, 4)]:
                    src = srcd.ap().rearrange("(dc p) o -> p dc o", p=128)
                    dst = bass.AP(wkaT, dc0 * 1024 + colh * 512,
                                  [[8192, 128], [1024, 4], [1, 512]])
                    sync.dma_start(out=dst, in_=src).then_inc(sem["qm2"], 16)
                for j in range(NT):
                    if j < 4:
                        sync.wait_ge(sem["vPk"], j + 1)
                    else:
                        sync.wait_ge(sem["vBn2"], 2 * (j - 3))
                        sync.wait_ge(sem["vY1"], j - 3)
                    sync.dma_start(out=x2_slot(j), in_=x_d[j * 128:(j + 1) * 128, :]
                                   ).then_inc(sem[QX2[j % 4]], 16)
                sync.wait_ge(sem["qsta"], 16 * 8)
                sync.wait_ge(sem["qstb"], 16 * 8)

            @block.gpsimd
            def _(gpsimd):
                nqs = 0
                for dst, src in [
                    (smalls[:], smalls_d), (eye32[:], eye32_d), (eyebf[:], eyebf_d),
                    (onesbf[:], onesbf_d), (g1r[:], g1r_d), (g2r[:], g2r_d),
                    (g1i[:], g1i_d), (g2i[:], g2i_d), (bcat[:], bcat_d),
                ]:
                    gpsimd.dma_start(out=dst, in_=src).then_inc(sem["qs"], 16)
                    nqs += 16
                    gpsimd.wait_ge(sem["qs"], nqs)
                n_d1 = 0
                for h in range(2):
                    for p4 in range(4):
                        gidx = h * 4 + p4
                        gpsimd.wait_ge(sem["sMev"], mev_after[gidx])
                        slot = t4a if (p4 % 2 == 0) else t4b
                        for ti in range(2):
                            t = p4 * 2 + ti
                            srcs = [(slot, wrt_d)] if h == 0 else [(slot, wit_d), (t4n, witn_d)]
                            for (st_, dstd) in srcs:
                                for dpp in range(4):
                                    off = ((4 * t + dpp) * 16) * 512
                                    dst = bass.AP(dstd, off, [[16, 32], [512, 16], [1, 16]])
                                    src = st_[dpp * 32:(dpp + 1) * 32, ti * 256:(ti + 1) * 256]
                                    gpsimd.dma_start(out=dst, in_=src).then_inc(sem["qm1"], 16)
                                    n_d1 += 1
                        gpsimd.wait_ge(sem["qm1"], 16 * n_d1)
                        gpsimd.sem_inc(sem["qm1done"], 1)
                assert n_d1 == 96, n_d1
                gpsimd.wait_ge(sem["sKmm"], 32)
                for i in range(4):
                    gpsimd.dma_start(out=wkT.ap()[:, i * 2048:(i + 1) * 2048],
                                     in_=wv_d[:, i * 2048:(i + 1) * 2048]
                                     ).then_inc(sem["qwv"], 16)
                qg = [0]

                def tap(name, srcap, waitsem, waitval):
                    if name in dbg_out:
                        gpsimd.wait_ge(sem[waitsem], waitval)
                        gpsimd.dma_start(out=dbg_out[name], in_=srcap).then_inc(sem["qg"], 16)
                        qg[0] += 16
                        dbg_gate[name] = qg[0]
                        gpsimd.wait_ge(sem["qg"], qg[0])

                def gdma(dst, srcap):
                    gpsimd.dma_start(out=dst, in_=srcap).then_inc(sem["qg"], 16)
                    qg[0] += 16
                    gpsimd.wait_ge(sem["qg"], qg[0])

                tap("dbg_n", nfm.ap()[:], "sNev", 32)
                tap("dbg_k", U1ap[:, 0:16384], "sKev", 32)
                tap("dbg_a", abuf.ap()[:], "sAev", 32)
                tap("dbg_ta", sTA[:], "vTa", 1)
                gpsimd.wait_ge(sem["vTa"], 1)
                gdma(cca_in.ap(), sTA[:])
                gpsimd.collective_compute(
                    "AllGather", OP.bypass, ins=[cca_in.ap().opt()],
                    outs=[cca_out.ap().opt()], replica_groups=[list(range(NCORE))],
                ).then_inc(sem["cc"], 1)
                gpsimd.wait_ge(sem["cc"], 1)
                gdma(g_a[:], cca_out.ap().rearrange("(r p) f -> p r f", p=128))
                dbg_gate["__ga"] = qg[0]
                tap("dbg_v", U1ap[:, 16384:32768], "sVev", 32)
                tap("dbg_pa", sPa[:], "vPa", 1)
                gpsimd.wait_ge(sem["vTp"], 8)
                tap("dbg_pk", abuf.ap()[:], "vTp", 8)
                tap("dbg_tp", sTP[:], "vTp", 8)
                gdma(ccp_in.ap(), sTP[:])
                gpsimd.collective_compute(
                    "AllGather", OP.bypass, ins=[ccp_in.ap().opt()],
                    outs=[ccp_out.ap().opt()], replica_groups=[list(range(NCORE))],
                ).then_inc(sem["cc"], 1)
                gpsimd.wait_ge(sem["cc"], 2)
                gdma(g_p[:], ccp_out.ap().rearrange("(r p) f -> p r f", p=128))
                dbg_gate["__gp"] = qg[0]
                tap("dbg_pp", sPp[:], "vPp", 1)
                tap("dbg_o", abuf.ap()[:], "vO", 8)
                tap("dbg_m", wkaT.ap()[:], "qm2", 64)

            @block.scalar
            def _(scalar):
                scalar.dma_start(out=abuf.ap()[:, 0:8192], in_=wk_d
                                 ).then_inc(sem["qw"], 16)
                scalar.dma_start(out=abuf.ap()[:, 8192:16384], in_=wka_d
                                 ).then_inc(sem["sTx"], 16)
                scalar.wait_ge(sem["qs"], 16)
                sc = [0]
                def schain(ins):
                    sc[0] += 1
                    ins.then_inc(sem["sC"], 1)
                    scalar.wait_ge(sem["sC"], sc[0])
                def rs1(j):
                    scalar.wait_ge(sem["vMV"], j + 1)
                    schain(scalar.activation(slntmp[:, j:j + 1], smv[:, 2 * j + 1: 2 * j + 2],
                                      AF.Ln, bias=EPSC))
                    scalar.activation(srs1[:, j:j + 1], slntmp[:, j:j + 1],
                                      AF.Exp, scale=-0.5).then_inc(sem["sRs1"], 1)
                rs1(0)
                rs1(1)
                scalar.wait_ge(sem["qw"], 16)
                for i in range(16):
                    scalar.activation(wkT.ap()[:, i * 512:(i + 1) * 512],
                                      abuf.ap()[:, i * 512:(i + 1) * 512],
                                      AF.Identity, scale=1.0).then_inc(sem["sWev"], 1)
                    if i == 7:
                        rs1(2)
                        rs1(3)
                def emit_nev(g):
                    for fc in range(8):
                        k = 32 + g * 8 + fc
                        scalar.wait_ge(sem["sNT"], g * 8 + fc + 1)
                        scalar.activation(
                            nfm.ap()[:, fc * 2048 + g * 512: fc * 2048 + (g + 1) * 512],
                            HALVES_BF[k % 4](), AF.Identity, scale=GAM(fc), bias=BET(fc)).then_inc(sem["sNev"], 1)

                def emit_Kev(tg):
                    rs1_at = {(0, 0): 4, (0, 1): 5, (0, 3): 6, (0, 5): 7, (0, 7): 8,
                              (1, 0): 9, (1, 2): 10, (1, 4): 11,
                              (2, 0): 12, (2, 2): 13, (2, 4): 14, (2, 6): 15}
                    for ec in range(8):
                        if (tg, ec) in rs1_at:
                            rs1(rs1_at[(tg, ec)])
                        gi = tg * 8 + ec
                        scalar.wait_ge(sem["sKmm"], gi + 1)
                        ps = pf[:, 0:512] if gi % 2 == 0 else pf[:, 512:1024]
                        scalar.activation(kcol(ec, tg), ps, AF.Identity,
                                          scale=1.0, bias=BK(ec)).then_inc(sem["sKev"], 1)
                        if gi % 4 == 3:
                            gidx = gi // 4
                            h, p4 = divmod(gidx, 4)
                            scalar.wait_ge(sem["sM1"], gidx + 1)
                            if gidx >= 1:
                                scalar.wait_ge(sem["qm1done"], gidx)
                            dstslot = t4a if (p4 % 2 == 0) else t4b
                            ins_ = scalar.activation(dstslot[:], pm[:], AF.Identity, scale=1.0)
                            if h == 1:
                                scalar.activation(t4n[:], pm[:], AF.Identity, scale=-1.0
                                                  ).then_inc(sem["sMev"], 2)
                            else:
                                ins_.then_inc(sem["sMev"], 1)

                def emit_expev(tg):
                    if tg == 0:
                        scalar.wait_ge(sem["vInit"], 1)
                        scalar.wait_ge(sem["sWev"], 32)
                    for fc in range(8):
                        gi = tg * 8 + fc
                        scalar.wait_ge(sem["sKAmm"], gi + 1)
                        ps = pf[:, 0:512] if gi % 2 == 0 else pf[:, 512:1024]
                        scalar.activation(
                            abuf.ap()[:, fc * 2048 + tg * 512: fc * 2048 + (tg + 1) * 512],
                            ps, AF.Exp, scale=1.0 / 32.0, bias=bka32[:, fc:fc + 1],
                            accum_out=expacc[:, fc * 4 + tg: fc * 4 + tg + 1]).then_inc(sem["sAev"], 1)

                for tg in range(4):
                    emit_nev(tg)
                    if tg == 1:
                        scalar.wait_ge(sem["sTx"], 16)
                        for i in range(16):
                            scalar.activation(wkaT.ap()[:, i * 512:(i + 1) * 512],
                                              abuf.ap()[:, 8192 + i * 512: 8192 + (i + 1) * 512],
                                              AF.Identity, scale=1.0).then_inc(sem["sWev"], 1)
                    emit_Kev(tg)
                    if tg >= 1:
                        emit_expev(tg - 1)
                emit_expev(3)
                scalar.wait_ge(sem["vNorm"], 16)
                scalar.wait_ge(sem["sNT"], 32)
                for gi in range(32):
                    tg, ec = divmod(gi, 8)
                    scalar.wait_ge(sem["sVmm"], gi + 1)
                    ps = pf[:, 0:512] if gi % 2 == 0 else pf[:, 512:1024]
                    scalar.activation(vcol(ec, tg), ps, AF.Identity,
                                      scale=1.0, bias=BV(ec)).then_inc(sem["sVev"], 1)
                for fc in range(8):
                    scalar.wait_ge(sem["vScanA"], fc + 1)
                    if fc == 0:
                        scalar.wait_ge(sem["vPa"], 1)
                    for hh in range(2):
                        asl = ascratch(fc)[:, hh * 1024:(hh + 1) * 1024]
                        schain(scalar.activation(asl, asl, AF.Ln,
                                                 bias=sPa[:, fc:fc + 1]))
                        scalar.activation(asl, asl, AF.Exp, scale=-1.0).then_inc(sem["sRA"], 1)
                for j in range(NT + 2):
                    if 1 <= j <= NT:
                        jj = j - 1
                        for hh in range(2):
                            scalar.wait_ge(sem["sY1T"], 2 * jj + hh + 1)
                            dst = bass.AP(nfm, (4 * hh) * 2048 + jj * 128,
                                          [[NC * T, 128], [2048, 4], [1, 128]])
                            srcb = py1 if hh == 0 else pm
                            scalar.activation(dst, srcb[:, 0:512], AF.Identity, scale=1.0).then_inc(sem["sY1ev"], 1)
                    if j < NT:
                        scalar.wait_ge(sem["vMV2"], j + 1)
                        schain(scalar.activation(slntmp[:, j:j + 1], smv2[:, 2 * j + 1: 2 * j + 2],
                                          AF.Ln, bias=EPSC))
                        scalar.activation(srs2[:, j:j + 1], slntmp[:, j:j + 1],
                                          AF.Exp, scale=-0.5).then_inc(sem["sRs2"], 1)
                    if j >= 2:
                        jj = j - 2
                        scalar.wait_ge(sem["vMV3"], jj + 1)
                        schain(scalar.activation(slntmp[:, jj:jj + 1], smv3[:, 2 * jj + 1: 2 * jj + 2],
                                          AF.Ln, bias=EPSC))
                        scalar.activation(srs3[:, jj:jj + 1], slntmp[:, jj:jj + 1],
                                          AF.Exp, scale=-0.5).then_inc(sem["sRs3"], 1)
                    if j >= 3:
                        jj = j - 3
                        scalar.wait_ge(sem["vOut"], jj + 1)
                        scalar.dma_start(out=out_d[jj * 128:(jj + 1) * 128, :], in_=out_slot(jj)
                                         ).then_inc(sem[QST[jj % 2]], 16)
                for jj in range(NT - 1, NT):
                    scalar.wait_ge(sem["vOut"], jj + 1)
                    scalar.dma_start(out=out_d[jj * 128:(jj + 1) * 128, :], in_=out_slot(jj)
                                     ).then_inc(sem[QST[jj % 2]], 16)

            @block.tensor
            def _(tensor):
                def tx_wait(k):
                    if k >= 4:
                        s, v = txlog[k - 4]
                        tensor.wait_ge(sem[s], v)

                tensor.wait_ge(sem["qs"], 48)
                evlog = []

                def pp_wait(g_global):
                    if g_global >= 2:
                        sname, v = evlog[g_global - 2]
                        tensor.wait_ge(sem[sname], v)

                nproj = [0]

                def emit_nT(g):
                    tensor.wait_ge(sem["vNorm"], 4 * (g + 1))
                    for fc in range(8):
                        k = g * 8 + fc
                        tx_wait(k)
                        half = HALVES_BF[k % 4]()
                        for i in range(4):
                            j = 4 * g + i
                            ins_ = tensor.transpose(half[:, i * 128:(i + 1) * 128],
                                             n_slot(j)[:, fc * 128:(fc + 1) * 128],
                                             eyebf[:])
                            if i == 3:
                                ins_.then_inc(sem["sNT"], 1)
                        txlog.append(("sNev", g * 8 + fc + 1))

                def emit_K(tg):
                    if tg == 0:
                        tensor.wait_ge(sem["sWev"], 16)
                    for ec in range(8):
                        gi = tg * 8 + ec
                        tensor.wait_ge(sem["sNev"], 8 * (tg + 1))
                        pp_wait(nproj[0])
                        nproj[0] += 1
                        ps = pf[:, 0:512] if gi % 2 == 0 else pf[:, 512:1024]
                        for dc in range(8):
                            ins_ = tensor.matmul(ps,
                                          wkT.ap()[:, dc * 1024 + ec * 128: dc * 1024 + (ec + 1) * 128],
                                          nfm.ap()[:, dc * 2048 + tg * 512: dc * 2048 + (tg + 1) * 512],
                                          start=(dc == 0), stop=(dc == 7))
                            if dc == 7:
                                ins_.then_inc(sem["sKmm"], 1)
                        evlog.append(("sKev", gi + 1))
                        if gi % 4 == 3:
                            gidx = gi // 4
                            if gidx == 0:
                                tensor.wait_ge(sem["qs"], 16 * 9)
                            if gidx > 0:
                                tensor.wait_ge(sem["sMev"], mev_after[gidx - 1])
                            h, p4 = divmod(gidx, 4)
                            sg1, sg2 = (g1r, g2r) if h == 0 else (g1i, g2i)
                            for ti in range(2):
                                t = p4 * 2 + ti
                                ins_ = tensor.matmul(pm[:, ti * 256:(ti + 1) * 256],
                                              sg1[0:4, t * 128:(t + 1) * 128], sg2[0:4, :],
                                              start=True, stop=True)
                                if ti == 1:
                                    ins_.then_inc(sem["sM1"], 1)

                def emit_KA(tg):
                    if tg == 0:
                        tensor.wait_ge(sem["sWev"], 32)
                    for fc in range(8):
                        gi = tg * 8 + fc
                        tensor.wait_ge(sem["sKev"], 8 * (tg + 1))
                        pp_wait(nproj[0])
                        nproj[0] += 1
                        ps = pf[:, 0:512] if gi % 2 == 0 else pf[:, 512:1024]
                        for ec in range(8):
                            ins_ = tensor.matmul(ps,
                                          wkaT.ap()[:, ec * 1024 + fc * 128: ec * 1024 + (fc + 1) * 128],
                                          kcol(ec, tg),
                                          start=(ec == 0), stop=(ec == 7))
                            if ec == 7:
                                ins_.then_inc(sem["sKAmm"], 1)
                        evlog.append(("sAev", gi + 1))

                for tg in range(4):
                    emit_nT(tg)
                    emit_K(tg)
                    if tg >= 1:
                        emit_KA(tg - 1)
                emit_KA(3)
                tensor.wait_ge(sem["qwv"], 64)
                for gi in range(32):
                    tg, ec = divmod(gi, 8)
                    pp_wait(64 + gi)
                    ps = pf[:, 0:512] if gi % 2 == 0 else pf[:, 512:1024]
                    for dc in range(8):
                        ins_ = tensor.matmul(ps,
                                      wkT.ap()[:, dc * 1024 + ec * 128: dc * 1024 + (ec + 1) * 128],
                                      nfm.ap()[:, dc * 2048 + tg * 512: dc * 2048 + (tg + 1) * 512],
                                      start=(dc == 0), stop=(dc == 7))
                        if dc == 7:
                            ins_.then_inc(sem["sVmm"], 1)
                    evlog.append(("sVev", gi + 1))
                tensor.wait_ge(sem["sMev"], mev_after[-1])
                tensor.wait_ge(sem["qm2"], 64)
                for lad, tot in [("vScanA", 8), ("sRA", 16), ("vPk", 8), ("vO", 7)]:
                    for i in range(1, tot + 1):
                        tensor.wait_ge(sem[lad], i)
                        for _ in range(3):
                            tensor.matmul(pm[:], eyebf[:], wkaT.ap()[:, 0:512],
                                          start=True, stop=True)
                def emit_oT(k):
                    if k > 0:
                        tensor.wait_ge(sem["vR1"], k)
                    for fc in range(8):
                        if k == 0:
                            tensor.wait_ge(sem["vO"], fc + 1)
                        ins_ = tensor.transpose(pt0[:, fc * 128:(fc + 1) * 128],
                                         abuf.ap()[:, fc * 2048 + k * 128: fc * 2048 + k * 128 + 128],
                                         eye32[:])
                        if fc == 7:
                            ins_.then_inc(sem["sOT"], 1)

                def emit_y1T(j):
                    tensor.wait_ge(sem["vY1"], j + 1)
                    for hh in range(2):
                        if 2 * j + hh >= 2:
                            tensor.wait_ge(sem["sY1ev"], 2 * j + hh - 1)
                        bank = py1 if hh == 0 else pm
                        for i in range(4):
                            dcc = hh * 4 + i
                            ins_ = tensor.transpose(bank[:, i * 128:(i + 1) * 128],
                                             y1f_slot(j)[:, dcc * 128:(dcc + 1) * 128],
                                             eye32[:])
                            if i == 3:
                                ins_.then_inc(sem["sY1T"], 1)

                def emit_FFN(j):
                    tensor.wait_ge(sem["sY1ev"], 2 * j + 1)
                    if j == 0:
                        tensor.wait_ge(sem["sVev"], 32)
                        tensor.wait_ge(sem["qm2"], 64)
                        tensor.wait_ge(sem["qs"], 16 * 9)
                    elif j >= 2:
                        tensor.wait_ge(sem["vR2"], j - 1)
                    pfj = pf if j % 2 == 0 else pt1
                    for oc in range(2):
                        for dcc in range(8):
                            if oc == 0 and dcc == 4:
                                tensor.wait_ge(sem["sY1ev"], 2 * (j + 1))
                            tensor.matmul(pfj[:, oc * 512:(oc + 1) * 512],
                                          nfm.ap()[:, dcc * 2048 + j * 128: dcc * 2048 + j * 128 + 128],
                                          wkaT.ap()[:, dcc * 1024 + oc * 512: dcc * 1024 + (oc + 1) * 512],
                                          start=(dcc == 0), stop=False)
                        ins_ = tensor.matmul(pfj[:, oc * 512:(oc + 1) * 512],
                                      onesbf[0:1, :], bcat[0:1, oc * 512:(oc + 1) * 512],
                                      start=False, stop=True)
                        if oc == 1:
                            ins_.then_inc(sem["sFmm"], 1)

                emit_oT(0)
                for j in range(NT):
                    if j + 1 < NT:
                        emit_oT(j + 1)
                    emit_y1T(j)
                    emit_FFN(j)
                    for lad, v in [("vOut", j - 1), ("vR1", j + 1), ("vMV2", j + 1), ("vY1", j + 1)]:
                        if 1 <= v <= NT:
                            tensor.wait_ge(sem[lad], v)
                            for _ in range(3):
                                tensor.ldweights(eyebf[:])

            @block.vector
            def _(vector):
                vc = [0]
                def vchain(ins):
                    vc[0] += 1
                    ins.then_inc(sem["vC"], 1)
                    vector.wait_ge(sem["vC"], vc[0])
                vector.wait_ge(sem["qs"], 16)
                vector.tensor_scalar(bka32[:], smalls[:, 24:32], 1.0 / 32.0, None,
                                     op0=OP.mult).then_inc(sem["vInit"], 1)
                for j in range(NT):
                    vector.wait_ge(sem[QX[j % 6]], 16 * (j // 6 + 1))
                    vector.bn_stats(bnst[:, 0:6], xin_slot(j)[:, 0:512]).then_inc(sem["vBn"], 1)
                    vector.bn_stats(bnst[:, 6:12], xin_slot(j)[:, 512:1024]).then_inc(sem["vBn"], 1)
                    vector.wait_ge(sem["vBn"], 2 * (j + 1))
                    vector.bn_aggr(smv[:, 2 * j: 2 * j + 2], bnst[:]).then_inc(sem["vMV"], 1)
                    vector.wait_ge(sem["sRs1"], j + 1)
                    if j >= 4:
                        vector.wait_ge(sem["sNT"], 8 * (j // 4))
                    vector.tensor_scalar(n_slot(j), xin_slot(j),
                                         smv[:, 2 * j: 2 * j + 1], srs1[:, j:j + 1],
                                         op0=OP.subtract, op1=OP.mult).then_inc(sem["vNorm"], 1)
                vector.wait_ge(sem["sAev"], 32)
                vector.tensor_reduce(sTA[:], expacc[:].rearrange("p (f g) -> p f g", f=8),
                                     axis=AX.X, op=OP.add).then_inc(sem["vTa"], 1)
                vector.wait_ge(sem["qg"], dbg_gate["__ga"])
                vchain(vector.tensor_tensor(g_m[:], g_a[:], MASK, op=OP.mult))
                vector.tensor_reduce(sPa[:], g_m[:].rearrange("p (r f) -> p f r", r=8),
                                     axis=AX.X, op=OP.add).then_inc(sem["vPa"], 1)

                def aslice(fc):
                    return abuf.ap()[:, fc * 2048:(fc + 1) * 2048]

                def scan_a(fc):
                    if fc == 0:
                        vector.wait_ge(sem["sVmm"], 32)
                    vector.tensor_tensor_scan(ascratch(fc), aslice(fc), aslice(fc),
                                              0.0, op0=OP.add, op1=OP.bypass
                                              ).then_inc(sem["vScanA"], 1)

                def u_pk(fc):
                    if fc == 0 and "dbg_a" in dbg_gate:
                        vector.wait_ge(sem["qg"], dbg_gate["dbg_a"])
                    for hh in range(2):
                        vector.wait_ge(sem["sRA"], 2 * fc + hh + 1)
                        vchain(vector.tensor_tensor(
                            aslice(fc)[:, hh * 1024:(hh + 1) * 1024],
                            aslice(fc)[:, hh * 1024:(hh + 1) * 1024],
                            ascratch(fc)[:, hh * 1024:(hh + 1) * 1024], op=OP.mult))
                    vchain(vector.tensor_tensor(aslice(fc), aslice(fc),
                                                U1ap[:, fc * 2048:(fc + 1) * 2048],
                                                op=OP.mult))
                    vector.tensor_tensor_scan(aslice(fc), aslice(fc), aslice(fc),
                                              0.0, op0=OP.add, op1=OP.bypass
                                              ).then_inc(sem["vPk"], 1)
                    vector.wait_ge(sem["vPk"], fc + 1)
                    vector.tensor_copy(sTP[:, fc:fc + 1],
                                       abuf.ap()[:, fc * 2048 + 2047: fc * 2048 + 2048]
                                       ).then_inc(sem["vTp"], 1)

                def o1(fc):
                    if fc == 0:
                        vector.wait_ge(sem["sVev"], 32)
                        if "dbg_pk" in dbg_gate:
                            vector.wait_ge(sem["qg"], dbg_gate["dbg_pk"])
                    vector.wait_ge(sem["vTp"], fc + 1)
                    vector.tensor_tensor(aslice(fc), aslice(fc),
                                         U1ap[:, 16384 + fc * 2048: 16384 + (fc + 1) * 2048],
                                         op=OP.mult).then_inc(sem["vSP"], 1)

                scan_a(0)
                for fc in range(1, 8):
                    scan_a(fc)
                    u_pk(fc - 1)
                u_pk(7)
                for fc in range(8):
                    o1(fc)
                vector.wait_ge(sem["qg"], dbg_gate["__gp"])
                vchain(vector.tensor_tensor(g_m[:], g_p[:], MASK, op=OP.mult))
                vector.tensor_reduce(sPp[:], g_m[:].rearrange("p (r f) -> p f r", r=8),
                                     axis=AX.X, op=OP.add).then_inc(sem["vPp"], 1)
                vector.wait_ge(sem["vPp"], 1)
                for fc in range(8):
                    vector.wait_ge(sem["vSP"], fc + 1)
                    vector.scalar_tensor_tensor(aslice(fc),
                                                U1ap[:, 16384 + fc * 2048: 16384 + (fc + 1) * 2048],
                                                sPp[:, fc:fc + 1], aslice(fc),
                                                op0=OP.mult, op1=OP.add).then_inc(sem["vO"], 1)
                for j in range(NT + 3):
                    if j >= 3:
                        jj = j - 3
                        vector.wait_ge(sem["sRs3"], jj + 1)
                        if jj >= 2:
                            vector.wait_ge(sem[QST[jj % 2]], 16 * (jj // 2))
                        vector.tensor_scalar(out_slot(jj), y1f_slot(jj),
                                             smv3[:, 2 * jj: 2 * jj + 1], srs3[:, jj:jj + 1],
                                             op0=OP.subtract, op1=OP.mult).then_inc(sem["vOut"], 1)
                    if 1 <= j <= NT:
                        jj = j - 1
                        vector.wait_ge(sem["sRs2"], jj + 1)
                        if jj >= 2:
                            vector.wait_ge(sem["vOut"], jj - 1)
                        vector.tensor_scalar(y1f_slot(jj), x2_slot(jj),
                                             smv2[:, 2 * jj: 2 * jj + 1], srs2[:, jj:jj + 1],
                                             op0=OP.subtract, op1=OP.mult).then_inc(sem["vY1"], 1)
                    if j < NT:
                        vector.wait_ge(sem["sOT"], j + 1)
                        vector.wait_ge(sem[QX2[j % 4]], 16 * (j // 4 + 1))
                        vector.scalar_tensor_tensor(x2_slot(j), pt0[:, 0:1024], 0.0, x2_slot(j),
                                                    op0=OP.add, op1=OP.add).then_inc(sem["vR1"], 1)
                        vector.wait_ge(sem["vR1"], j + 1)
                        vector.bn_stats(bnst2[:, 0:6], x2_slot(j)[:, 0:512]).then_inc(sem["vBn2"], 1)
                        vector.bn_stats(bnst2[:, 6:12], x2_slot(j)[:, 512:1024]).then_inc(sem["vBn2"], 1)
                        vector.wait_ge(sem["vBn2"], 2 * (j + 1))
                        vector.bn_aggr(smv2[:, 2 * j: 2 * j + 2], bnst2[:]).then_inc(sem["vMV2"], 1)
                    if 2 <= j < NT + 2:
                        jj = j - 2
                        pfj = pf if jj % 2 == 0 else pt1
                        vector.wait_ge(sem["sFmm"], jj + 1)
                        vector.scalar_tensor_tensor(y1f_slot(jj), pfj[:, 0:1024], 0.0, y1f_slot(jj),
                                                    op0=OP.add, op1=OP.add).then_inc(sem["vR2"], 1)
                        vector.wait_ge(sem["vR2"], jj + 1)
                        vector.bn_stats(bnst3[:, 0:6], y1f_slot(jj)[:, 0:512]).then_inc(sem["vBn3"], 1)
                        vector.bn_stats(bnst3[:, 6:12], y1f_slot(jj)[:, 512:1024]).then_inc(sem["vBn3"], 1)
                        vector.wait_ge(sem["vBn3"], 2 * (jj + 1))
                        vector.bn_aggr(smv3[:, 2 * jj: 2 * jj + 2], bnst3[:]).then_inc(sem["vMV3"], 1)

    return nc


def kernel(L0_attn_hidden_states, L0_attn_attention_mask, L0_attn_Wk, L0_attn_bk,
           L0_attn_Wka, L0_attn_bka, L0_attn_Wv, L0_attn_bv, L0_attn_gamma,
           L0_attn_beta, L0_ffn_G1_R, L0_ffn_G2_R, L0_ffn_G1_I, L0_ffn_G2_I,
           L0_ffn_bias_real, L0_ffn_bias_imag):
    x = np.ascontiguousarray(
        np.asarray(L0_attn_hidden_states, dtype=np.float32).reshape(NCORE * T, D))
    def wprep(w):
        return np.ascontiguousarray(
            np.asarray(w, dtype=np.float32).T.reshape(8, 128, 1024)
            .transpose(1, 0, 2).reshape(128, 8192))
    wk = wprep(L0_attn_Wk)
    wka = wprep(L0_attn_Wka)
    wv = wprep(L0_attn_Wv)

    def col8(v):
        return np.ascontiguousarray(np.asarray(v, dtype=np.float32).reshape(8, 128).T)

    smalls = np.zeros((128, 112), dtype=np.float32)
    smalls[:, 104] = EPS
    smalls[:, 0:8] = col8(L0_attn_gamma)
    smalls[:, 8:16] = col8(L0_attn_beta)
    smalls[:, 16:24] = col8(L0_attn_bk)
    smalls[:, 24:32] = col8(L0_attn_bka)
    smalls[:, 32:40] = col8(L0_attn_bv)

    g1r = np.ascontiguousarray(
        np.asarray(L0_ffn_G1_R, np.float32)[0].transpose(1, 2, 0).reshape(4, 1024))
    g2r = np.ascontiguousarray(
        np.asarray(L0_ffn_G2_R, np.float32)[:, :, 0, :].transpose(0, 2, 1).reshape(4, 256))
    g1i = np.ascontiguousarray(
        np.asarray(L0_ffn_G1_I, np.float32)[0].transpose(1, 2, 0).reshape(4, 1024))
    g2i = np.ascontiguousarray(
        np.asarray(L0_ffn_G2_I, np.float32)[:, :, 0, :].transpose(0, 2, 1).reshape(4, 256))
    bcat = np.ascontiguousarray(np.concatenate([
        np.asarray(L0_ffn_bias_real, np.float32),
        np.asarray(L0_ffn_bias_imag, np.float32)]).reshape(1, 1024))

    eye32 = np.eye(128, dtype=np.float32)
    eyebf = np.eye(128).astype(ml_dtypes.bfloat16)
    onesbf = np.ones((1, 128), dtype=np.float32).astype(ml_dtypes.bfloat16)

    nc = build_nc()
    in_maps = []
    for c in range(NCORE):
        m = np.zeros((8, 8), dtype=np.float32)
        m[:c, :] = 1.0
        sm = smalls.copy()
        sm[:, 40:104] = np.broadcast_to(m.reshape(1, 64), (128, 64))
        in_maps.append({
            "x": np.ascontiguousarray(x[c * T:(c + 1) * T]),
            "wk": wk, "wka": wka, "wv": wv,
            "smalls": sm, "eye32": eye32, "eyebf": eyebf, "onesbf": onesbf,
            "g1r": g1r, "g2r": g2r, "g1i": g1i, "g2i": g2i, "bcat": bcat,
        })
    res = run_bass_kernel_spmd(nc, in_maps, core_ids=list(range(NCORE)))
    out = np.concatenate([np.asarray(r["out"]) for r in res.results], axis=0)
    return out.reshape(1, NCORE * T, D)
